# version 01
# baseline (speedup 1.0000x reference)
import contextlib
import numpy as np
import concourse.bass as bass
import concourse.mybir as mybir
from concourse.bass_utils import run_bass_kernel_spmd

F32 = mybir.dt.float32
BF16 = mybir.dt.bfloat16
I32 = mybir.dt.int32
AF = mybir.ActivationFunctionType
ALU = mybir.AluOpType

T = 2048
D = 4096
NCH, CHT = 16, 128
FBS = [128] * 80 + [64]
NFB = len(FBS)

CN_M1, CN_P1, CN_I512, CN_GNE, CN_MH, CN_I4096, CN_LNE, CN_Z = range(8)


class Sem:
    def __init__(self, nc, name):
        self.h = nc.alloc_semaphore(name)
        self.v = 0

    def inc(self, instr, n=1):
        instr.then_inc(self.h, n)
        self.v += n
        return instr


class Prog:
    def __init__(self, nc):
        self.nc = nc
        self.guards = {}
        self.waited = {}

    def wait(self, eng, sem, v=None):
        v = sem.v if v is None else v
        if v <= 0:
            return
        key = (id(eng), id(sem))
        if self.waited.get(key, -1) >= v:
            return
        eng.wait_ge(sem.h, v)
        self.waited[key] = v

    def guard(self, eng, key):
        g = self.guards.get(key)
        if g is not None:
            self.wait(eng, g[0], g[1])

    def consumed(self, key, sem, v=None):
        self.guards[key] = (sem, sem.v if v is None else v)


def build_nc(dump=False, stop_after='F', bnch=None, bstage=None, evar=0):
    nc = bass.Bass("TRN2", target_bir_lowering=False)
    di = lambda n, s, d=F32: nc.dram_tensor(n, s, d, kind="ExternalInput").ap()

    xT = di("xT", [D, T])
    winT = di("winT", [D, 10304])
    woutT = di("woutT", [D, D])
    wffnT = di("wffnT", [D, D])
    acatT = di("acatT", [D, 48])
    bcatT = di("bcatT", [48, D])
    fbias = di("fbias", [1, D])
    convw_r = di("convw_r", [128, 48 * 4])
    convb_r = di("convb_r", [128, 48])
    dtbias = di("dtbias", [64, 1])
    alog = di("alog", [64, 1])
    dvec = di("dvec", [1, 64])
    nw_r = di("nw_r", [128, 32])
    lsi = di("lsi", [4, 1], I32)
    bid = di("bid", [1, 1])
    identf = di("identf", [128, 128])
    identb = di("identb", [128, 128], BF16)
    ut01 = di("ut01", [128, 128])
    negut = di("negut", [128, 128])
    onesf = di("onesf", [1, 512])
    onesb = di("onesb", [1, 512], BF16)
    onescb = di("onescb", [128, 1], BF16)
    jrow = di("jrow", [1, 3])
    cn = di("cn", [128, 8])

    outT = nc.dram_tensor("outT", [D, T], F32, kind="ExternalOutput").ap()
    ksp = "ExternalOutput" if dump else "Internal"
    projT = nc.dram_tensor("projT", [10240, T], BF16, kind=ksp).ap()
    dt_f = nc.dram_tensor("dt_f", [64, T], F32, kind=ksp).ap()
    ynT_d = nc.dram_tensor("ynT_d", [D, T], BF16, kind=ksp).ap()
    moT_d = nc.dram_tensor("moT_d", [D, T], BF16, kind=ksp).ap()
    x1_d = nc.dram_tensor("x1_d", [D, T], BF16, kind=ksp).ap()
    ffnT_d = nc.dram_tensor("ffnT_d", [D, T], BF16, kind=ksp).ap()
    flat_d = nc.dram_tensor("flat_d", [2, 64, 128], F32, kind="Internal").ap()
    d_s4 = nc.dram_tensor("d_s4", [4], F32, kind="Internal").ap()
    d_m48 = nc.dram_tensor("d_m48", [48], F32, kind="Internal").ap()
    xb_d = nc.dram_tensor("xb_d", [32, 128, T], BF16, kind=ksp).ap()
    dbg_d = nc.dram_tensor("dbg_d", [128, 8192], F32, kind=ksp).ap()
    dbgb_d = nc.dram_tensor("dbgb_d", [128, 8192], BF16, kind=ksp).ap()

    P = Prog(nc)
    sy, gp, pe, dv, ac = nc.sync, nc.gpsimd, nc.tensor, nc.vector, nc.scalar

    names = ["i_ld", "i_pe", "i_dv", "i_ac",
             "a_x", "a_w0", "a_w1", "a_mm", "a_ev", "a_st0", "a_st1",
             "b_ld", "b_fl", "b_pe", "b_dv", "b_ac", "b_po", "b_st0", "b_st1",
             "c_ld", "c_w0", "c_w1", "c_mm", "c_ev", "c_st0", "c_st1",
             "d_ld", "d_pe", "d_dv", "d_ac",
             "e_w0", "e_w1", "e_mm", "e_dv", "e_ev", "e_st0", "e_st1",
             "f_ld", "f_pe", "f_dv", "f_ac", "f_st"]
    S = {k: Sem(nc, k) for k in names}

    sbt = nc.alloc_sbuf_tensor
    c_identf = sbt("c_identf", [128, 128], F32)
    c_identb = sbt("c_identb", [128, 128], BF16)
    c_ut01 = sbt("c_ut01", [128, 128], F32)
    c_negut = sbt("c_negut", [128, 128], F32)
    c_onesf = sbt("c_onesf", [1, 512], F32)
    c_onesb = sbt("c_onesb", [1, 512], BF16)
    c_onescb = sbt("c_onescb", [128, 1], BF16)
    c_jrow = sbt("c_jrow", [1, 3], F32)
    c_cn = sbt("c_cn", [128, 8], F32)
    c_convw = sbt("c_convw", [128, 48 * 4], F32)
    c_convb = sbt("c_convb", [128, 48], F32)
    c_nw = sbt("c_nw", [128, 32], F32)
    c_dtb = sbt("c_dtb", [64, 1], F32)
    c_alog = sbt("c_alog", [64, 1], F32)
    c_acol = sbt("c_acol", [64, 1], F32)
    c_dbc = sbt("c_dbc", [128, 64], F32)
    c_mask = sbt("c_mask", [48, 1], F32)
    c_fbias = sbt("c_fbias", [1, D], BF16)

    def cnc(col, p=128):
        return c_cn[0:p, col:col + 1]

    with nc.Block():
        ac.activation(c_cn[0:1, 7:8], c_cn[0:1, 7:8], AF.Copy)
        ac.activation(c_fbias[0:1, 0:1], c_cn[0:1, 7:8], AF.Silu)
        dv.memset(c_fbias[0:1, 0:2], 0.0)
        dv.tensor_copy(c_fbias[0:1, 2:3], c_cn[0:1, 7:8])
        dv.tensor_scalar(c_cn[0:1, 7:8], c_cn[0:1, 7:8], 0.0, 1.0, ALU.mult, ALU.min)
        gp.memset(c_fbias[0:1, 4:6], 0.0)
        with nc.sbuf_tensor("i_t4", [4, 4], F32) as i_t4, \
             nc.sbuf_tensor("i_lsi", [4, 1], I32) as i_lsi, \
             nc.sbuf_tensor("i_t1", [1, 64], F32) as i_t1, \
             nc.psum_tensor("i_ps", [128, 512], F32) as i_ps:
            for dst, src_ in [(c_identf, identf), (c_identb, identb), (c_ut01, ut01),
                              (c_negut, negut), (c_onesf, onesf), (c_onesb, onesb),
                              (c_onescb, onescb), (c_jrow, jrow), (c_cn, cn),
                              (c_convw, convw_r), (c_convb, convb_r), (c_nw, nw_r),
                              (c_dtb, dtbias), (c_alog, alog), (i_lsi, lsi)]:
                S["i_ld"].inc(sy.dma_start(dst[:], src_), 16)
            S["i_ld"].inc(sy.dma_start(c_dbc[0:1, :], dvec), 16)
            for k in range(4):
                S["i_ld"].inc(sy.dma_start(i_t4[k:k + 1, 0:1], bid), 16)
            S["i_ld"].inc(gp.dma_start(c_fbias[:], fbias), 16)
            P.wait(ac, S["i_ld"])
            ac.activation(c_acol[:], c_alog[:], AF.Exp)
            ac.nop(cycle_cnt=512, nofuse=True)
            S["i_ac"].inc(ac.activation(c_acol[:], c_acol[:], AF.Copy, scale=cnc(CN_M1, 64)))
            P.wait(pe, S["i_ld"])
            S["i_pe"].inc(pe.matmul(i_ps[:, 0:64], c_onesf[0:1, 0:128], c_dbc[0:1, :], start=True, stop=True))
            P.wait(dv, S["i_pe"])
            S["i_dv"].inc(dv.tensor_copy(c_dbc[:], i_ps[:, 0:64]))
            P.consumed(("initps",), S["i_dv"])
            P.wait(dv, S["i_ld"])
            def dvop(fn, *a):
                fn(*a)
                dv.nop(cycle_cnt=512, nofuse=True)
            dvop(dv.tensor_copy, i_t4[:, 1:2], i_lsi[:])
            dvop(dv.tensor_tensor, i_t4[:, 2:3], i_t4[:, 1:2], i_t4[:, 0:1], ALU.subtract)
            dvop(dv.tensor_scalar, i_t4[:, 2:3], i_t4[:, 2:3], -1.0, None, ALU.mult)
            dvop(dv.tensor_scalar, i_t4[:, 2:3], i_t4[:, 2:3], 1.0, None, ALU.add)
            dvop(dv.tensor_scalar, i_t4[:, 2:3], i_t4[:, 2:3], 1.0, None, ALU.min)
            dv.tensor_scalar(i_t4[:, 2:3], i_t4[:, 2:3], 0.0, None, ALU.max)
            S["i_dv"].inc(dv.nop(cycle_cnt=512, nofuse=True))
            P.wait(sy, S["i_dv"])
            S["i_ld"].inc(sy.dma_start(d_s4, i_t4[:, 2:3]), 16)
            S["i_ld"].inc(sy.dma_start(i_t1[0:1, 0:4], d_s4.unsqueeze(0)), 16)
            P.wait(dv, S["i_ld"])
            dvop(dv.tensor_reduce, i_t1[0:1, 4:5], i_t1[0:1, 0:4], mybir.AxisListType.X, ALU.add)
            dvop(dv.tensor_scalar, i_t1[0:1, 5:6], i_t1[0:1, 4:5], -1.0, None, ALU.add)
            dvop(dv.tensor_scalar, i_t1[0:1, 5:6], i_t1[0:1, 5:6], 1.0, None, ALU.min)
            dvop(dv.tensor_scalar, i_t1[0:1, 5:6], i_t1[0:1, 5:6], 0.0, None, ALU.max)
            dvop(dv.tensor_scalar, i_t1[0:1, 6:7], i_t1[0:1, 4:5], -2.0, None, ALU.add)
            dvop(dv.tensor_scalar, i_t1[0:1, 6:7], i_t1[0:1, 6:7], 2.0, None, ALU.min)
            dvop(dv.tensor_scalar, i_t1[0:1, 6:7], i_t1[0:1, 6:7], 0.0, None, ALU.max)
            dv.tensor_scalar(i_t1[0:1, 8:11], c_jrow[:], i_t1[0:1, 6:7], None, ALU.subtract)
            S["i_dv"].inc(dv.nop(cycle_cnt=512, nofuse=True))
            P.wait(ac, S["i_dv"])
            S["i_ac"].inc(ac.activation(i_t1[0:1, 8:11], i_t1[0:1, 8:11], AF.Abs))
            P.wait(dv, S["i_ac"])
            dvop(dv.tensor_scalar, i_t1[0:1, 8:11], i_t1[0:1, 8:11], -1.0, None, ALU.mult)
            dvop(dv.tensor_scalar, i_t1[0:1, 8:11], i_t1[0:1, 8:11], 1.0, None, ALU.add)
            dvop(dv.tensor_scalar, i_t1[0:1, 8:11], i_t1[0:1, 8:11], 1.0, None, ALU.min)
            dvop(dv.tensor_scalar, i_t1[0:1, 8:11], i_t1[0:1, 8:11], 0.0, None, ALU.max)
            dvop(dv.tensor_scalar, i_t1[0:1, 8:11], i_t1[0:1, 8:11], i_t1[0:1, 5:6], None, ALU.mult)
            dv.tensor_copy(i_t1[0:1, 16:64].rearrange("p (j r) -> p j r", r=16),
                           i_t1[0:1, 8:11].unsqueeze(2).broadcast_to([1, 3, 16]))
            S["i_dv"].inc(dv.nop(cycle_cnt=512, nofuse=True))
            P.wait(sy, S["i_dv"])
            S["i_ld"].inc(sy.dma_start(d_m48, i_t1[0:1, 16:64]), 16)
            S["i_ld"].inc(sy.dma_start(c_mask[:], d_m48.unsqueeze(1)), 16)
            P.wait(dv, S["i_ld"])
            S["i_dv"].inc(dv.tensor_copy(i_t4[:, 3:4], i_t4[:, 2:3]))

        with nc.sbuf_tensor("a_xb", [128, 32 * T], BF16) as a_xb, \
             nc.sbuf_tensor("a_ws0", [128, 32 * 128], BF16) as a_ws0, \
             nc.sbuf_tensor("a_ws1", [128, 32 * 128], BF16) as a_ws1, \
             nc.sbuf_tensor("a_ev0", [128, T], BF16) as a_ev0, \
             nc.sbuf_tensor("a_ev1", [128, T], BF16) as a_ev1, \
             nc.sbuf_tensor("a_evdt", [64, T], F32) as a_evdt, \
             nc.psum_tensor("a_ps", [128, 8 * 512], F32) as a_ps:
            a_ws = [a_ws0, a_ws1]
            a_ev = [a_ev0, a_ev1]
            xTv = xT.rearrange("(kb p) t -> p kb t", p=128)
            P.wait(gp, S["i_dv"])
            P.wait(gp, S["i_ac"])
            P.wait(gp, S["i_pe"])
            for i in range(4):
                S["a_x"].inc(gp.dma_start(
                    a_xb[:].rearrange("p (kb t) -> p kb t", t=T)[:, i * 8:(i + 1) * 8, :],
                    xTv[:, i * 8:(i + 1) * 8, :]), 16)
            winTv = winT.rearrange("(kb p) f -> p kb f", p=128)
            for fb in range(NFB):
                pi = fb % 2
                fsz = FBS[fb]
                f0 = fb * 128
                P.guard(gp, ("aws", pi))
                wsv = a_ws[pi][:].rearrange("p (kb f) -> p kb f", f=128)[:, :, 0:fsz]
                S["a_w" + str(pi)].inc(gp.dma_start(wsv, winTv[:, :, f0:f0 + fsz]), 16)
                P.wait(pe, S["a_x"], 64)
                P.wait(pe, S["a_w" + str(pi)])
                P.guard(pe, ("apq", pi))
                P.guard(pe, ("initps",))
                mm = None
                for kb in range(32):
                    for ts4 in range(T // 512):
                        mm = pe.matmul(a_ps[0:fsz, (pi * 4 + ts4) * 512:(pi * 4 + ts4 + 1) * 512],
                                       wsv[:, kb, :],
                                       a_xb[:, kb * T + ts4 * 512: kb * T + (ts4 + 1) * 512],
                                       start=(kb == 0), stop=(kb == 31))
                S["a_mm"].inc(mm)
                P.consumed(("aws", pi), S["a_mm"])
                P.wait(ac, S["a_mm"])
                P.guard(ac, ("aev", pi))
                cp = None
                for ts4 in range(T // 512):
                    src = a_ps[0:fsz, (pi * 4 + ts4) * 512:(pi * 4 + ts4 + 1) * 512]
                    if fb < 80:
                        cp = ac.copy(a_ev[pi][:, ts4 * 512:(ts4 + 1) * 512], src)
                    else:
                        cp = ac.copy(a_evdt[:, ts4 * 512:(ts4 + 1) * 512], src)
                S["a_ev"].inc(cp)
                P.consumed(("apq", pi), S["a_ev"])
                P.wait(sy, S["a_ev"])
                if fb < 80:
                    S["a_st" + str(pi)].inc(sy.dma_start(projT[f0:f0 + 128, :], a_ev[pi][:]), 16)
                else:
                    S["a_st" + str(pi)].inc(sy.dma_start(dt_f[:, :], a_evdt[:]), 16)
                P.consumed(("aev", pi), S["a_st" + str(pi)])

        if stop_after == "A":
            if dump:
                P.wait(sy, S["a_mm"])
                S["a_st0"].inc(sy.dma_start(xb_d.rearrange("kb p t -> p kb t"),
                                            a_xb[:].rearrange("p (kb t) -> p kb t", t=T)), 16)
            P.wait(sy, S["a_st0"])
            P.wait(sy, S["a_st1"])
            return nc
        with contextlib.ExitStack() as _stk:
            b_hv = _stk.enter_context(nc.sbuf_tensor("b_hv", [128, 48 * 131], BF16))
            b_accA = _stk.enter_context(nc.sbuf_tensor("b_accA", [128, 48 * 64], F32))
            b_accB = _stk.enter_context(nc.sbuf_tensor("b_accB", [128, 48 * 64], F32))
            b_xbc0 = _stk.enter_context(nc.sbuf_tensor("b_xbc0", [128, 48 * 128], BF16))
            b_xbc1 = _stk.enter_context(nc.sbuf_tensor("b_xbc1", [128, 48 * 128], BF16))
            b_gsil = _stk.enter_context(nc.sbuf_tensor("b_gsil", [128, 32 * 128], BF16))
            b_dtv0 = _stk.enter_context(nc.sbuf_tensor("b_dtv0", [64, 128], F32))
            b_dtv1 = _stk.enter_context(nc.sbuf_tensor("b_dtv1", [64, 128], F32))
            b_dts0 = _stk.enter_context(nc.sbuf_tensor("b_dts0", [64, 128], F32))
            b_dts1 = _stk.enter_context(nc.sbuf_tensor("b_dts1", [64, 128], F32))
            b_dts2 = _stk.enter_context(nc.sbuf_tensor("b_dts2", [64, 128], F32))
            b_dtp0 = _stk.enter_context(nc.sbuf_tensor("b_dtp0", [128, 64], F32))
            b_dtp1 = _stk.enter_context(nc.sbuf_tensor("b_dtp1", [128, 64], F32))
            b_lgt = _stk.enter_context(nc.sbuf_tensor("b_lgt", [128, 64], F32))
            b_cumt0 = _stk.enter_context(nc.sbuf_tensor("b_cumt0", [128, 64], F32))
            b_cumt1 = _stk.enter_context(nc.sbuf_tensor("b_cumt1", [128, 64], F32))
            b_cumT0 = _stk.enter_context(nc.sbuf_tensor("b_cumT0", [64, 128], F32))
            b_cumT1 = _stk.enter_context(nc.sbuf_tensor("b_cumT1", [64, 128], F32))
            b_lam0 = _stk.enter_context(nc.sbuf_tensor("b_lam0", [128, 64], F32))
            b_lam1 = _stk.enter_context(nc.sbuf_tensor("b_lam1", [128, 64], F32))
            b_ccb = _stk.enter_context(nc.sbuf_tensor("b_ccb", [128, 64], F32))
            b_wtm = _stk.enter_context(nc.sbuf_tensor("b_wtm", [128, 64], F32))
            b_dec0 = _stk.enter_context(nc.sbuf_tensor("b_dec0", [128, 64], F32))
            b_dec1 = _stk.enter_context(nc.sbuf_tensor("b_dec1", [128, 64], F32))
            b_wdt0 = _stk.enter_context(nc.sbuf_tensor("b_wdt0", [128, 64], F32))
            b_wdt1 = _stk.enter_context(nc.sbuf_tensor("b_wdt1", [128, 64], F32))
            b_flat = _stk.enter_context(nc.sbuf_tensor("b_flat", [1, 8192], F32))
            b_mq0 = _stk.enter_context(nc.sbuf_tensor("b_mq0", [128, 512], F32))
            b_mq1 = _stk.enter_context(nc.sbuf_tensor("b_mq1", [128, 512], F32))
            b_mg0 = _stk.enter_context(nc.sbuf_tensor("b_mg0", [128, 1024], BF16))
            b_mg1 = _stk.enter_context(nc.sbuf_tensor("b_mg1", [128, 1024], BF16))
            b_dtx0 = _stk.enter_context(nc.sbuf_tensor("b_dtx0", [128, 512], BF16))
            b_dtx1 = _stk.enter_context(nc.sbuf_tensor("b_dtx1", [128, 512], BF16))
            b_dxs0 = _stk.enter_context(nc.sbuf_tensor("b_dxs0", [128, 512], BF16))
            b_dxs1 = _stk.enter_context(nc.sbuf_tensor("b_dxs1", [128, 512], BF16))
            b_Dx0 = _stk.enter_context(nc.sbuf_tensor("b_Dx0", [128, 512], BF16))
            b_Dx1 = _stk.enter_context(nc.sbuf_tensor("b_Dx1", [128, 512], BF16))
            b_ysc = _stk.enter_context(nc.sbuf_tensor("b_ysc", [128, 512], F32))
            b_Yg0 = _stk.enter_context(nc.sbuf_tensor("b_Yg0", [128, 512], F32))
            b_Yg1 = _stk.enter_context(nc.sbuf_tensor("b_Yg1", [128, 512], F32))
            b_btok0 = _stk.enter_context(nc.sbuf_tensor("b_btok0", [128, 1024], BF16))
            b_btok1 = _stk.enter_context(nc.sbuf_tensor("b_btok1", [128, 1024], BF16))
            b_S = _stk.enter_context(nc.sbuf_tensor("b_S", [128, 8 * 512], F32))
            b_Sb = _stk.enter_context(nc.sbuf_tensor("b_Sb", [128, 8 * 512], BF16))
            b_ss = _stk.enter_context(nc.sbuf_tensor("b_ss", [128, 16], F32))
            b_sq = _stk.enter_context(nc.sbuf_tensor("b_sq", [128, 512], F32))
            b_yn0 = _stk.enter_context(nc.sbuf_tensor("b_yn0", [128, 32 * 128], BF16))
            b_yt0 = _stk.enter_context(nc.sbuf_tensor("b_yt0", [128, 32 * 128], BF16))
            b_yt1 = _stk.enter_context(nc.sbuf_tensor("b_yt1", [128, 32 * 128], BF16))
            b_pf = _stk.enter_context(nc.psum_tensor("b_pf", [128, 6 * 512], F32))
            b_pt0 = _stk.enter_context(nc.psum_tensor("b_pt0", [128, 1024], BF16))
            b_pt1 = _stk.enter_context(nc.psum_tensor("b_pt1", [128, 1024], BF16))
            b_dtv = [b_dtv0, b_dtv1]
            b_dtp = [b_dtp0, b_dtp1]
            b_cumt = [b_cumt0, b_cumt1]
            b_cumT = [b_cumT0, b_cumT1]
            b_lam = [b_lam0, b_lam1]
            b_dec = [b_dec0, b_dec1]
            b_wdt = [b_wdt0, b_wdt1]
            b_xbc = [b_xbc0, b_xbc1]
            b_mq = [b_mq0, b_mq1]
            b_mg = [b_mg0, b_mg1]
            b_dtx = [b_dtx0, b_dtx1]
            b_dxs = [b_dxs0, b_dxs1]
            b_Dx = [b_Dx0, b_Dx1]
            b_Yg = [b_Yg0, b_Yg1]
            b_btok = [b_btok0, b_btok1]
            b_yn = [b_yn0, b_yn0]
            b_yt = [b_yt0, b_yt1]

            def pf(i):
                return b_pf[:, i * 512:(i + 1) * 512]

            projv = projT.rearrange("(blk p) t -> p blk t", p=128)

            P.wait(dv, S["a_st0"])
            P.wait(dv, S["a_st1"])
            dv.memset(b_S[:], 0.0)
            S["b_dv"].inc(dv.memset(
                b_hv[:].rearrange("p (blk w) -> p blk w", w=131)[:, :, 0:3], 0.0))

            for c in range(NCH if bnch is None else bnch):
                pi = c % 2
                t0 = c * CHT
                P.wait(sy, S["a_st0"])
                P.wait(sy, S["a_st1"])
                P.guard(sy, ("hv",))
                hvv = b_hv[:].rearrange("p (blk w) -> p blk w", w=131)
                if c == 0:
                    P.wait(sy, S["b_dv"])
                    S["b_ld"].inc(sy.dma_start(hvv[:, :, 3:131], projv[:, 32:80, 0:128]), 16)
                else:
                    S["b_ld"].inc(sy.dma_start(hvv[:, :, 0:131], projv[:, 32:80, t0 - 3:t0 + 128]), 16)
                P.guard(sy, ("gsil",))
                S["b_ld"].inc(sy.dma_start(b_gsil[:].rearrange("p (blk t) -> p blk t", t=128),
                                           projv[:, 0:32, t0:t0 + 128]), 16)
                P.guard(sy, ("dtv", pi))
                S["b_ld"].inc(sy.dma_start(b_dtv[pi][:], dt_f[:, t0:t0 + 128]), 16)
                ld_now = S["b_ld"].v

                P.wait(dv, S["b_ld"], ld_now)
                S["b_dv"].inc(dv.tensor_scalar(b_dts0[:], b_dtv[pi][:], c_dtb[:], None, ALU.add))
                P.consumed(("dtv", pi), S["b_dv"])
                P.wait(ac, S["b_dv"])
                ac.activation(b_dts1[:], b_dts0[:], AF.Abs)
                ac.activation(b_dts1[:], b_dts1[:], AF.Exp, scale=cnc(CN_M1, 64))
                S["b_ac"].inc(ac.activation(b_dts1[:], b_dts1[:], AF.Ln, bias=cnc(CN_P1, 64)))
                P.wait(dv, S["b_ac"])
                dv.tensor_scalar(b_dts2[:], b_dts0[:], 0.0, None, ALU.max)
                dv.tensor_tensor(b_dts2[:], b_dts2[:], b_dts1[:], ALU.add)
                S["b_dv"].inc(dv.tensor_scalar(b_dts0[:], b_dts2[:], c_acol[:], None, ALU.mult))
                if bstage == 1:
                    P.wait(sy, S["b_dv"])
                    sdmp = Sem(nc, "sdmp")
                    sdmp.inc(sy.dma_start(dbg_d[0:64, 0:128], b_dts2[:]), 16)
                    sdmp.inc(sy.dma_start(dbg_d[0:64, 128:256], b_dts0[:]), 16)
                    sy.wait_ge(sdmp.h, 32)
                    return nc
                P.guard(pe, ("pf", 0))
                P.wait(pe, S["b_dv"])
                pe.transpose(b_pf[0:128, 0:64], b_dts2[:], c_identf[0:64, 0:64])
                S["b_pe"].inc(pe.transpose(b_pf[0:128, 64:128], b_dts0[:], c_identf[0:64, 0:64]))
                P.wait(dv, S["b_pe"])
                P.guard(dv, ("dtp", pi))
                dv.tensor_copy(b_dtp[pi][:], b_pf[0:128, 0:64])
                S["b_dv"].inc(dv.tensor_copy(b_lgt[:], b_pf[0:128, 64:128]))
                P.consumed(("pf", 0), S["b_dv"])
                P.guard(pe, ("pf", 1))
                P.wait(pe, S["b_dv"])
                pe.matmul(b_pf[:, 512:576], c_ut01[:], b_lgt[:], start=True, stop=True)
                S["b_pe"].inc(pe.matmul(b_pf[0:64, 576:704], b_lgt[:], c_ut01[:], start=True, stop=True))
                P.wait(dv, S["b_pe"])
                P.guard(dv, ("cum", pi))
                dv.tensor_copy(b_cumt[pi][:], b_pf[:, 512:576])
                S["b_dv"].inc(dv.tensor_copy(b_cumT[pi][:], b_pf[0:64, 576:704]))
                P.consumed(("pf", 1), S["b_dv"])
                P.wait(sy, S["b_dv"])
                S["b_fl"].inc(sy.dma_start(flat_d[pi], b_cumT[pi][:]), 16)
                P.guard(sy, ("flat",))
                S["b_fl"].inc(sy.dma_start(b_flat[:], flat_d[pi].rearrange("h t -> (h t)").unsqueeze(0)), 16)
                fl_now = S["b_fl"].v
                P.wait(ac, S["b_dv"])
                P.guard(ac, ("lam", pi))
                S["b_ac"].inc(ac.activation(b_lam[pi][:], b_cumt[pi][:], AF.Exp))
                P.guard(pe, ("pf", 0))
                P.wait(pe, S["b_fl"], fl_now)
                S["b_pe"].inc(pe.matmul(b_pf[:, 0:64], c_onesf[0:1, 0:128],
                                        b_flat[0:1, 127::128], start=True, stop=True))
                P.wait(dv, S["b_pe"])
                S["b_dv"].inc(dv.tensor_copy(b_ccb[:], b_pf[:, 0:64]))
                P.consumed(("pf", 0), S["b_dv"])
                P.wait(ac, S["b_dv"])
                P.guard(ac, ("dec", pi))
                S["b_ac"].inc(ac.activation(b_dec[pi][:], b_ccb[:], AF.Exp))
                S["b_dv"].inc(dv.tensor_tensor(b_wtm[:], b_ccb[:], b_cumt[pi][:], ALU.subtract))
                P.wait(ac, S["b_dv"])
                S["b_ac"].inc(ac.activation(b_wtm[:], b_wtm[:], AF.Exp))
                P.wait(dv, S["b_ac"])
                P.guard(dv, ("wdt", pi))
                S["b_dv"].inc(dv.tensor_tensor(b_wdt[pi][:], b_wtm[:], b_dtp[pi][:], ALU.mult))

                if bstage == 2:
                    P.wait(sy, S["b_dv"])
                    P.wait(sy, S["b_ac"])
                    sdmp = Sem(nc, "sdmp")
                    sdmp.inc(sy.dma_start(dbg_d[:, 0:64], b_cumt[pi][:]), 16)
                    sdmp.inc(sy.dma_start(dbg_d[0:64, 64:192], b_cumT[pi][:]), 16)
                    sdmp.inc(sy.dma_start(dbg_d[:, 192:256], b_lam[pi][:]), 16)
                    sdmp.inc(sy.dma_start(dbg_d[:, 256:320], b_ccb[:]), 16)
                    sdmp.inc(sy.dma_start(dbg_d[:, 320:384], b_dec[pi][:]), 16)
                    sdmp.inc(sy.dma_start(dbg_d[:, 384:448], b_wdt[pi][:]), 16)
                    sdmp.inc(sy.dma_start(dbg_d[0:1, 448:8192], b_flat[0:1, 0:7744]), 16)
                    sy.wait_ge(sdmp.h, 112)
                    return nc
                wk = c_convw[:].rearrange("p (blk k) -> p blk k", k=4)
                cbc = c_convb[:].unsqueeze(2).broadcast_to([128, 48, 64])
                for half in range(2):
                    hw = lambda k: hvv[:, :, k + half * 64: k + half * 64 + 64]
                    av = b_accA[:].rearrange("p (blk w) -> p blk w", w=64)
                    bv = b_accB[:].rearrange("p (blk w) -> p blk w", w=64)
                    wb = lambda k: wk[:, :, k:k + 1].broadcast_to([128, 48, 64])
                    P.wait(dv, S["b_ld"], ld_now)
                    P.wait(gp, S["b_ld"], ld_now)
                    P.guard(gp, ("accB",))
                    S["b_po"].inc(gp.tensor_tensor(bv, hw(1), wb(1), ALU.mult))
                    P.guard(dv, ("accA",))
                    S["b_dv"].inc(dv.tensor_tensor(av, hw(0), wb(0), ALU.mult))
                    P.wait(dv, S["b_po"])
                    S["b_dv"].inc(dv.tensor_tensor(av, av, bv, ALU.add))
                    P.wait(gp, S["b_dv"])
                    S["b_po"].inc(gp.tensor_tensor(bv, hw(2), wb(2), ALU.mult))
                    P.wait(dv, S["b_po"])
                    S["b_dv"].inc(dv.tensor_tensor(av, av, bv, ALU.add))
                    P.wait(gp, S["b_dv"])
                    S["b_po"].inc(gp.tensor_tensor(bv, hw(3), wb(3), ALU.mult))
                    P.wait(dv, S["b_po"])
                    dv.tensor_tensor(av, av, bv, ALU.add)
                    S["b_dv"].inc(dv.tensor_tensor(av, av, cbc, ALU.add))
                    P.consumed(("accB",), S["b_dv"])
                    P.wait(ac, S["b_dv"])
                    P.guard(ac, ("xbc", pi))
                    xv = b_xbc[pi][:].rearrange("p (blk t) -> p blk t", t=128)
                    S["b_ac"].inc(ac.activation(xv[:, :, half * 64:(half + 1) * 64], av, AF.Silu))
                    P.consumed(("accA",), S["b_ac"])
                P.consumed(("hv",), S["b_dv"])
                P.wait(ac, S["b_ld"], ld_now)
                S["b_ac"].inc(ac.activation(b_gsil[:], b_gsil[:], AF.Silu))
                silu_done = S["b_ac"].v

                if bstage == 3:
                    P.wait(sy, S["b_ac"])
                    sdmp = Sem(nc, "sdmp")
                    sdmp.inc(sy.dma_start(dbgb_d[:, 0:6144], b_xbc[pi][:]), 16)
                    sdmp.inc(sy.dma_start(dbgb_d[:, 6144:6144 + 2048], b_gsil[:, 0:2048]), 16)
                    sy.wait_ge(sdmp.h, 32)
                    return nc
                xbcv = b_xbc[pi][:].rearrange("p (blk t) -> p blk t", t=128)
                P.wait(pe, S["b_ac"], silu_done)
                P.guard(pe, ("pt0",))
                tr = None
                for g in range(8):
                    tr = pe.transpose(b_pt0[:, g * 128:(g + 1) * 128], xbcv[:, 32 + g, :], c_identb[:])
                S["b_pe"].inc(tr)
                P.wait(dv, S["b_pe"])
                P.guard(dv, ("btok", pi))
                S["b_dv"].inc(dv.tensor_copy(b_btok[pi][:], b_pt0[:]))
                P.consumed(("pt0",), S["b_dv"])

                if bstage == 4:
                    P.wait(sy, S["b_dv"])
                    sdmp = Sem(nc, "sdmp")
                    sdmp.inc(sy.dma_start(dbgb_d[:, 0:1024], b_btok[pi][:]), 16)
                    sy.wait_ge(sdmp.h, 16)
                    return nc
                ngr = 8 if (bstage is None or bstage < 5) else (bstage - 50 if bstage > 50 else 8)
                for g in range(ngr):
                    gi = g % 2
                    P.guard(pe, ("pt1",))
                    tr = None
                    for i4 in range(4):
                        tr = pe.transpose(b_pt1[:, i4 * 128:(i4 + 1) * 128],
                                          xbcv[:, 4 * g + i4, :], c_identb[:])
                    S["b_pe"].inc(tr)
                    P.guard(pe, ("pf", 2 + gi))
                    S["b_pe"].inc(pe.matmul(pf(2 + gi)[:, 0:128], xbcv[:, 32 + g, :],
                                            xbcv[:, 40 + g, :], start=True, stop=True))
                    for qi in range(2):
                        q = 2 * g + qi
                        P.guard(pe, ("pf", qi))
                        S["b_pe"].inc(pe.matmul(pf(qi)[:, 0:512], c_onesf[0:1, 0:128],
                                                b_flat[0:1, q * 512:(q + 1) * 512], start=True, stop=True))
                    if g == 7:
                        P.consumed(("flat",), S["b_pe"])
                    pe_now = S["b_pe"].v
                    P.wait(dv, S["b_pe"], pe_now)
                    P.guard(dv, ("dtx", gi))
                    ptv = b_pt1[:, 0:512].rearrange("p (h x) -> p h x", x=64)
                    dbc8 = lambda t: t.unsqueeze(2).broadcast_to([128, 8, 64])
                    dv.tensor_tensor(b_dtx[gi][:].rearrange("p (h x) -> p h x", x=64),
                                     ptv, dbc8(b_dtp[pi][:, g * 8:(g + 1) * 8]), ALU.mult)
                    dv.tensor_tensor(b_dxs[gi][:].rearrange("p (h x) -> p h x", x=64),
                                     ptv, dbc8(b_wdt[pi][:, g * 8:(g + 1) * 8]), ALU.mult)
                    S["b_dv"].inc(dv.tensor_tensor(b_Dx[gi][:].rearrange("p (h x) -> p h x", x=64),
                                                   ptv, dbc8(c_dbc[:, g * 8:(g + 1) * 8]), ALU.mult))
                    P.consumed(("pt1",), S["b_dv"])
                    if g == 7:
                        P.consumed(("dtp", pi), S["b_dv"])
                        P.consumed(("wdt", pi), S["b_dv"])
                    for qi in range(2):
                        P.guard(dv, ("mq", qi))
                        st = None
                        for hh in range(4):
                            h = (2 * g + qi) * 4 + hh
                            st = dv.scalar_tensor_tensor(
                                b_mq[qi][:, hh * 128:(hh + 1) * 128],
                                pf(qi)[:, hh * 128:(hh + 1) * 128],
                                b_cumt[pi][:, h:h + 1], c_negut[:],
                                ALU.subtract, ALU.min)
                        S["b_dv"].inc(st)
                        P.consumed(("pf", qi), S["b_dv"])
                        P.wait(ac, S["b_dv"])
                        S["b_ac"].inc(ac.activation(b_mq[qi][:], b_mq[qi][:], AF.Exp))
                        P.wait(dv, S["b_ac"])
                        P.guard(dv, ("mg", gi, qi))
                        S["b_dv"].inc(dv.tensor_tensor(
                            b_mg[gi][:, qi * 512:(qi + 1) * 512].rearrange("p (h t) -> p h t", t=128),
                            b_mq[qi][:].rearrange("p (h t) -> p h t", t=128),
                            pf(2 + gi)[:, 0:128].unsqueeze(1).broadcast_to([128, 4, 128]),
                            ALU.mult))
                        P.consumed(("mq", qi), S["b_dv"])
                    P.consumed(("pf", 2 + gi), S["b_dv"])
                    mg_done = S["b_dv"].v
                    P.wait(pe, S["b_dv"], mg_done)
                    P.guard(pe, ("pf", 4))
                    mm = None
                    for h8 in range(8):
                        mm = pe.matmul(pf(4)[:, h8 * 64:(h8 + 1) * 64],
                                       b_mg[gi][:, h8 * 128:(h8 + 1) * 128],
                                       b_dtx[gi][:, h8 * 64:(h8 + 1) * 64], start=True, stop=True)
                    S["b_pe"].inc(mm)
                    if c > 0:
                        P.guard(pe, ("pf", 5))
                        mm = None
                        for h8 in range(8):
                            mm = pe.matmul(pf(5)[:, h8 * 64:(h8 + 1) * 64],
                                           xbcv[:, 40 + g, :],
                                           b_Sb[:, g * 512 + h8 * 64: g * 512 + (h8 + 1) * 64],
                                           start=True, stop=True)
                        S["b_pe"].inc(mm)
                    P.guard(pe, ("pf", 0))
                    mm = None
                    for h8 in range(8):
                        mm = pe.matmul(pf(0)[:, h8 * 64:(h8 + 1) * 64],
                                       b_btok[pi][:, g * 128:(g + 1) * 128],
                                       b_dxs[gi][:, h8 * 64:(h8 + 1) * 64], start=True, stop=True)
                    S["b_pe"].inc(mm)
                    ymm_done = S["b_pe"].v
                    if g == 7:
                        P.consumed(("btok", pi), S["b_pe"])
                    P.wait(dv, S["b_pe"], ymm_done)
                    P.guard(dv, ("Yg", gi))
                    lam8 = b_lam[pi][:, g * 8:(g + 1) * 8].unsqueeze(2).broadcast_to([128, 8, 64])
                    if c > 0:
                        dv.tensor_tensor(b_ysc[:].rearrange("p (h x) -> p h x", x=64),
                                         pf(5)[:, 0:512].rearrange("p (h x) -> p h x", x=64),
                                         lam8, ALU.mult)
                        dv.tensor_tensor(b_ysc[:], b_ysc[:], pf(4)[:, 0:512], ALU.add)
                        S["b_dv"].inc(dv.tensor_tensor(b_Yg[gi][:], b_ysc[:], b_Dx[gi][:], ALU.add))
                    else:
                        S["b_dv"].inc(dv.tensor_tensor(b_Yg[gi][:], pf(4)[:, 0:512], b_Dx[gi][:], ALU.add))
                    P.consumed(("pf", 4), S["b_dv"])
                    P.consumed(("pf", 5), S["b_dv"])
                    P.consumed(("dtx", gi), S["b_dv"])
                    if g == 7:
                        P.consumed(("lam", pi), S["b_dv"])
                    dec8 = b_dec[pi][:, g * 8:(g + 1) * 8].unsqueeze(2).broadcast_to([128, 8, 64])
                    Sg = b_S[:, g * 512:(g + 1) * 512]
                    if c > 0:
                        dv.tensor_tensor(Sg.rearrange("p (h x) -> p h x", x=64),
                                         Sg.rearrange("p (h x) -> p h x", x=64), dec8, ALU.mult)
                        dv.tensor_tensor(Sg, Sg, pf(0)[:, 0:512], ALU.add)
                    else:
                        dv.tensor_copy(Sg, pf(0)[:, 0:512])
                    S["b_dv"].inc(dv.tensor_copy(b_Sb[:, g * 512:(g + 1) * 512], Sg))
                    P.consumed(("pf", 0), S["b_dv"])
                    if g == 7:
                        P.consumed(("dec", pi), S["b_dv"])
                    P.wait(ac, S["b_dv"])
                    ac.activation(b_sq[:].rearrange("p (a b) -> p a b", a=1),
                                  b_Yg[gi][:].rearrange("p (a b) -> p a b", a=1),
                                  AF.Square, accum_out=b_ss[:, g:g + 1])
                    S["b_ac"].inc(ac.activation(b_ss[:, g + 8:g + 9], b_ss[:, g:g + 1],
                                                AF.Ln, scale=cnc(CN_I512), bias=cnc(CN_GNE)))
                    P.wait(dv, S["b_ac"])
                    S["b_dv"].inc(dv.tensor_scalar(b_ss[:, g + 8:g + 9], b_ss[:, g + 8:g + 9],
                                                   -0.5, None, ALU.mult))
                    P.wait(ac, S["b_dv"])
                    S["b_ac"].inc(ac.activation(b_ss[:, g + 8:g + 9], b_ss[:, g + 8:g + 9], AF.Exp))
                    P.wait(dv, S["b_ac"])
                    P.guard(dv, ("yn", pi, g))
                    S["b_dv"].inc(dv.tensor_scalar(b_yn[pi][:, g * 512:(g + 1) * 512], b_Yg[gi][:],
                                                   b_ss[:, g + 8:g + 9], None, ALU.mult))
                    P.consumed(("Yg", gi), S["b_dv"])
                    P.wait(pe, S["b_dv"])
                    P.guard(pe, ("pt0",))
                    tr = None
                    for i4 in range(4):
                        blk = 4 * g + i4
                        tr = pe.transpose(b_pt0[:, i4 * 128:(i4 + 1) * 128],
                                          b_yn[pi][:, blk * 128:(blk + 1) * 128], c_identb[:])
                    S["b_pe"].inc(tr)
                    P.wait(dv, S["b_pe"])
                    st = None
                    for i4 in range(4):
                        blk = 4 * g + i4
                        P.guard(dv, ("yt", pi, blk))
                        st = dv.scalar_tensor_tensor(
                            b_yt[pi][:, blk * 128:(blk + 1) * 128],
                            b_pt0[:, i4 * 128:(i4 + 1) * 128],
                            c_nw[:, blk:blk + 1],
                            b_gsil[:, blk * 128:(blk + 1) * 128],
                            ALU.mult, ALU.mult)
                    S["b_dv"].inc(st)
                    P.consumed(("pt0",), S["b_dv"])
                    if g == 7:
                        P.consumed(("gsil",), S["b_dv"])
                        P.consumed(("xbc", pi), S["b_dv"])
                        P.consumed(("cum", pi), S["b_dv"])
                if bstage is not None and bstage > 50:
                    P.wait(sy, S["b_dv"])
                    sdmp = Sem(nc, "sdmp")
                    gl = ngr - 1
                    sdmp.inc(sy.dma_start(dbg_d[:, 0:512], b_Yg[gl % 2][:]), 16)
                    sdmp.inc(sy.dma_start(dbgb_d[:, 0:1024], b_mg[gl % 2][:]), 16)
                    sdmp.inc(sy.dma_start(dbgb_d[:, 1024:1536], b_dtx[gl % 2][:]), 16)
                    sdmp.inc(sy.dma_start(dbgb_d[:, 1536:2048], b_dxs[gl % 2][:]), 16)
                    sdmp.inc(sy.dma_start(dbgb_d[:, 2048:2560], b_Dx[gl % 2][:]), 16)
                    sdmp.inc(sy.dma_start(dbg_d[:, 512:4608], b_S[:]), 16)
                    sdmp.inc(sy.dma_start(dbgb_d[:, 2560:2560 + 4096], b_yn[pi][:]), 16)
                    sdmp.inc(sy.dma_start(dbg_d[:, 4608:4624], b_ss[:]), 16)
                    sy.wait_ge(sdmp.h, 128)
                    return nc
                P.wait(sy, S["b_dv"])
                S["b_st" + str(pi)].inc(sy.dma_start(
                    ynT_d.rearrange("(blk p) t -> p blk t", p=128)[:, :, t0:t0 + 128],
                    b_yt[pi][:].rearrange("p (blk t) -> p blk t", t=128)), 16)
                for blk in range(32):
                    P.consumed(("yt", pi, blk), S["b_st" + str(pi)])
                for g in range(8):
                    P.consumed(("yn", pi, g), S["b_st" + str(pi)])

        if stop_after == "B":
            P.wait(sy, S["b_st0"])
            P.wait(sy, S["b_st1"])
            return nc
        with nc.sbuf_tensor("c_yb", [128, 32 * T], BF16) as c_yb, \
             nc.sbuf_tensor("c_ws0", [128, 32 * 128], BF16) as c_ws0, \
             nc.sbuf_tensor("c_ws1", [128, 32 * 128], BF16) as c_ws1, \
             nc.sbuf_tensor("c_ev0", [128, T], BF16) as c_ev0, \
             nc.sbuf_tensor("c_ev1", [128, T], BF16) as c_ev1, \
             nc.psum_tensor("c_ps", [128, 8 * 512], F32) as c_ps:
            c_ws = [c_ws0, c_ws1]
            c_ev = [c_ev0, c_ev1]
            ynv = ynT_d.rearrange("(kb p) t -> p kb t", p=128)
            P.wait(sy, S["b_st0"])
            P.wait(sy, S["b_st1"])
            for i in range(4):
                S["c_ld"].inc(sy.dma_start(
                    c_yb[:].rearrange("p (kb t) -> p kb t", t=T)[:, i * 8:(i + 1) * 8, :],
                    ynv[:, i * 8:(i + 1) * 8, :]), 16)
            woutv = woutT.rearrange("(kb p) f -> p kb f", p=128)
            for fb in range(32):
                pi = fb % 2
                f0 = fb * 128
                P.wait(gp, S["b_st0"])
                P.wait(gp, S["b_st1"])
                P.guard(gp, ("cws", pi))
                wsv = c_ws[pi][:].rearrange("p (kb f) -> p kb f", f=128)
                S["c_w" + str(pi)].inc(gp.dma_start(wsv, woutv[:, :, f0:f0 + 128]), 16)
                P.wait(pe, S["c_ld"], 64)
                P.wait(pe, S["c_w" + str(pi)])
                P.guard(pe, ("cpq", pi))
                mm = None
                for kb in range(32):
                    for ts4 in range(T // 512):
                        mm = pe.matmul(c_ps[:, (pi * 4 + ts4) * 512:(pi * 4 + ts4 + 1) * 512],
                                       wsv[:, kb, :],
                                       c_yb[:, kb * T + ts4 * 512: kb * T + (ts4 + 1) * 512],
                                       start=(kb == 0), stop=(kb == 31))
                S["c_mm"].inc(mm)
                P.consumed(("cws", pi), S["c_mm"])
                P.wait(ac, S["c_mm"])
                P.guard(ac, ("cev", pi))
                cp = None
                for ts4 in range(T // 512):
                    cp = ac.copy(c_ev[pi][:, ts4 * 512:(ts4 + 1) * 512],
                                 c_ps[:, (pi * 4 + ts4) * 512:(pi * 4 + ts4 + 1) * 512])
                S["c_ev"].inc(cp)
                P.consumed(("cpq", pi), S["c_ev"])
                P.wait(sy, S["c_ev"])
                S["c_st" + str(pi)].inc(sy.dma_start(moT_d[f0:f0 + 128, :], c_ev[pi][:]), 16)
                P.consumed(("cev", pi), S["c_st" + str(pi)])

        x1r = nc.alloc_sbuf_tensor("x1r", [128, 32 * T], BF16)
        x1rv = x1r[:].rearrange("p (kb t) -> p kb t", t=T)

        def ln_phase(src_d, resid, out_ap, out_f32, Sld, Spe, Sdv, Sac, Sst, pref, dump_ap):
            with contextlib.ExitStack() as _stk:
                p_mo = _stk.enter_context(nc.sbuf_tensor(pref + "_mo", [128, D], BF16))
                p_xt = _stk.enter_context(nc.sbuf_tensor(pref + "_xt", [128, D], F32)) if resid is not None else None
                p_xp = _stk.enter_context(nc.sbuf_tensor(pref + "_xp", [128, D], F32))
                p_xb = _stk.enter_context(nc.sbuf_tensor(pref + "_xb", [128, D], BF16))
                p_sq = _stk.enter_context(nc.sbuf_tensor(pref + "_sq", [128, D], BF16))
                p_mb = _stk.enter_context(nc.sbuf_tensor(pref + "_mb", [128, 128], F32))
                p_sm = _stk.enter_context(nc.sbuf_tensor(pref + "_sm", [1, 512], F32))
                p_oc = _stk.enter_context(nc.sbuf_tensor(pref + "_oc", [128, D], F32)) if out_f32 else None
                p_p0 = _stk.enter_context(nc.psum_tensor(pref + "_p0", [128, 512], F32))
                p_p1 = _stk.enter_context(nc.psum_tensor(pref + "_p1", [128, 512], F32))
                srcv = src_d.rearrange("(blk p) t -> p blk t", p=128)
                for c in range(NCH):
                    pi = c % 2
                    t0 = c * CHT
                    P.guard(sy, (pref + "mo",))
                    Sld.inc(sy.dma_start(p_mo[:].rearrange("p (blk t) -> p blk t", t=128),
                                         srcv[:, :, t0:t0 + 128]), 16)
                    if resid is not None:
                        P.guard(sy, (pref + "xt",))
                        Sld.inc(sy.dma_start(p_xt[:].rearrange("p (blk t) -> p blk t", t=128),
                                             resid.rearrange("(blk p) t -> p blk t", p=128)[:, :, t0:t0 + 128]), 16)
                    ld_now = Sld.v
                    P.wait(dv, Sld, ld_now)
                    P.guard(dv, (pref + "xp",))
                    if resid is not None:
                        Sdv.inc(dv.tensor_tensor(p_xp[:], p_xt[:], p_mo[:], ALU.add))
                        P.consumed((pref + "xt",), Sdv)
                    else:
                        Sdv.inc(dv.tensor_tensor(p_xp[:].rearrange("p (blk t) -> p blk t", t=128),
                                                 p_mo[:].rearrange("p (blk t) -> p blk t", t=128),
                                                 x1rv[:, :, t0:t0 + 128], ALU.add))
                    P.consumed((pref + "mo",), Sdv)
                    P.wait(ac, Sdv)
                    P.guard(ac, (pref + "xb",))
                    ac.activation(p_xb[:], p_xp[:], AF.Copy)
                    Sac.inc(ac.activation(p_sq[:], p_xp[:], AF.Square))
                    P.wait(pe, Sac)
                    P.guard(pe, (pref + "p0",))
                    for blk in range(32):
                        pe.matmul(p_p0[0:1, 0:128], c_onescb[:],
                                  p_xb[:, blk * 128:(blk + 1) * 128], start=(blk == 0), stop=(blk == 31))
                    mm = None
                    for blk in range(32):
                        mm = pe.matmul(p_p0[0:1, 128:256], c_onescb[:],
                                       p_sq[:, blk * 128:(blk + 1) * 128], start=(blk == 0), stop=(blk == 31))
                    Spe.inc(mm)
                    P.consumed((pref + "xb",), Spe)
                    P.wait(dv, Spe)
                    dv.tensor_scalar(p_sm[0:1, 0:128], p_p0[0:1, 0:128], 1.0 / 4096.0, None, ALU.mult)
                    dv.tensor_scalar(p_sm[0:1, 128:256], p_p0[0:1, 128:256], 1.0 / 4096.0, None, ALU.mult)
                    dv.tensor_tensor(p_sm[0:1, 256:384], p_sm[0:1, 0:128], p_sm[0:1, 0:128], ALU.mult)
                    Sdv.inc(dv.tensor_tensor(p_sm[0:1, 384:512], p_sm[0:1, 128:256],
                                             p_sm[0:1, 256:384], ALU.subtract))
                    P.consumed((pref + "p0",), Sdv)
                    P.wait(ac, Sdv)
                    Sac.inc(ac.activation(p_sm[0:1, 384:512], p_sm[0:1, 384:512],
                                          AF.Ln, bias=cnc(CN_LNE, 1)))
                    P.wait(dv, Sac)
                    Sdv.inc(dv.tensor_scalar(p_sm[0:1, 384:512], p_sm[0:1, 384:512], -0.5, None, ALU.mult))
                    P.wait(ac, Sdv)
                    Sac.inc(ac.activation(p_sm[0:1, 384:512], p_sm[0:1, 384:512], AF.Exp))
                    P.wait(pe, Sac)
                    P.guard(pe, (pref + "p1",))
                    pe.matmul(p_p1[:, 0:128], c_onesf[0:1, 0:128], p_sm[0:1, 0:128], start=True, stop=True)
                    Spe.inc(pe.matmul(p_p1[:, 128:256], c_onesf[0:1, 0:128], p_sm[0:1, 384:512],
                                      start=True, stop=True))
                    P.wait(dv, Spe)
                    dv.tensor_copy(p_mb[:], p_p1[:, 0:128])
                    dv.tensor_tensor(p_xp[:].rearrange("p (blk t) -> p blk t", t=128),
                                     p_xp[:].rearrange("p (blk t) -> p blk t", t=128),
                                     p_mb[:].unsqueeze(1).broadcast_to([128, 32, 128]), ALU.subtract)
                    if out_f32:
                        P.guard(dv, (pref + "oc",))
                        Sdv.inc(dv.tensor_tensor(p_oc[:].rearrange("p (blk t) -> p blk t", t=128),
                                                 p_xp[:].rearrange("p (blk t) -> p blk t", t=128),
                                                 p_p1[:, 128:256].unsqueeze(1).broadcast_to([128, 32, 128]),
                                                 ALU.mult))
                        P.consumed((pref + "p1",), Sdv)
                        P.consumed((pref + "xp",), Sdv)
                        P.wait(sy, Sdv)
                        Sst.inc(sy.dma_start(out_ap.rearrange("(blk p) t -> p blk t", p=128)[:, :, t0:t0 + 128],
                                             p_oc[:].rearrange("p (blk t) -> p blk t", t=128)), 16)
                        P.consumed((pref + "oc",), Sst)
                    else:
                        Sdv.inc(dv.tensor_tensor(out_ap[:, :, t0:t0 + 128],
                                                 p_xp[:].rearrange("p (blk t) -> p blk t", t=128),
                                                 p_p1[:, 128:256].unsqueeze(1).broadcast_to([128, 32, 128]),
                                                 ALU.mult))
                        P.consumed((pref + "p1",), Sdv)
                        P.consumed((pref + "xp",), Sdv)
                        if dump_ap is not None:
                            P.wait(sy, Sdv)
                            Sst.inc(sy.dma_start(
                                dump_ap.rearrange("(blk p) t -> p blk t", p=128)[:, :, t0:t0 + 128],
                                out_ap[:, :, t0:t0 + 128]), 16)

        if stop_after == "C":
            P.wait(sy, S["c_st0"])
            P.wait(sy, S["c_st1"])
            return nc
        for eng in (sy, dv, ac, pe):
            P.wait(eng, S["c_st0"])
            P.wait(eng, S["c_st1"])
        ln_phase(moT_d, xT, x1rv, False, S["d_ld"], S["d_pe"], S["d_dv"], S["d_ac"],
                 S["f_st"], "d", x1_d if dump else None)

        if stop_after == "D":
            P.wait(sy, S["f_st"])
            P.wait(sy, S["d_dv"])
            return nc
        with nc.sbuf_tensor("e_ws0", [128, 32 * 128], BF16) as e_ws0, \
             nc.sbuf_tensor("e_ws1", [128, 32 * 128], BF16) as e_ws1, \
             nc.sbuf_tensor("e_ev0", [128, T], BF16) as e_ev0, \
             nc.sbuf_tensor("e_ev1", [128, T], BF16) as e_ev1, \
             nc.sbuf_tensor("e_ac", [128, 32 * 48], BF16) as e_acat, \
             nc.sbuf_tensor("e_bc", [48, D], BF16) as e_bcat, \
             nc.sbuf_tensor("e_rb", [48, T], BF16) as e_rb, \
             nc.psum_tensor("e_ps", [128, 8 * 512], F32) as e_ps:
            e_ws = [e_ws0, e_ws1]
            e_ev = [e_ev0, e_ev1]
            dvdone = S["d_dv"].v
            P.wait(gp, S["d_dv"], dvdone)
            S["e_w0"].inc(gp.dma_start(e_acat[:].rearrange("p (kb j) -> p kb j", j=48),
                                       acatT.rearrange("(kb p) j -> p kb j", p=128)), 16)
            S["e_w0"].inc(gp.dma_start(e_bcat[:], bcatT), 16)
            P.wait(pe, S["e_w0"], 32)
            P.wait(pe, S["d_dv"], dvdone)
            for ts4 in range(T // 512):
                mm = None
                for kb in range(32):
                    mm = pe.matmul(e_ps[0:48, (4 + ts4) * 512:(4 + ts4 + 1) * 512],
                                   e_acat[:, kb * 48:(kb + 1) * 48],
                                   x1r[:, kb * T + ts4 * 512: kb * T + (ts4 + 1) * 512],
                                   start=(kb == 0), stop=(kb == 31))
                S["e_mm"].inc(mm)
            P.wait(dv, S["e_mm"])
            rr = None
            for ts4 in range(T // 512):
                rr = dv.tensor_scalar(e_rb[:, ts4 * 512:(ts4 + 1) * 512],
                                      e_ps[0:48, (4 + ts4) * 512:(4 + ts4 + 1) * 512],
                                      c_mask[:], None, ALU.mult)
            S["e_dv"].inc(rr)
            P.consumed(("epq", 1), S["e_dv"])
            if evar == 4:
                P.wait(sy, S["e_dv"])
                sdmp_e = Sem(nc, "sdmp_e")
                sdmp_e.inc(sy.dma_start(dbgb_d[0:48, 0:2048], e_rb[:]), 16)
                sy.wait_ge(sdmp_e.h, 16)
            wffnv = wffnT.rearrange("(kb p) f -> p kb f", p=128)
            for fb in range(32):
                pi = fb % 2
                f0 = fb * 128
                P.guard(gp, ("ews", pi))
                wsv = e_ws[pi][:].rearrange("p (kb f) -> p kb f", f=128)
                S["e_w" + str(pi)].inc(gp.dma_start(wsv, wffnv[:, :, f0:f0 + 128]), 16)
                P.wait(pe, S["e_w" + str(pi)])
                P.wait(pe, S["e_dv"], S["e_dv"].v)
                P.guard(pe, ("epq", pi))
                mm = None
                for kb in range(32):
                    for ts4 in range(T // 512):
                        mm = pe.matmul(e_ps[:, (pi * 4 + ts4) * 512:(pi * 4 + ts4 + 1) * 512],
                                       wsv[:, kb, :],
                                       x1r[:, kb * T + ts4 * 512: kb * T + (ts4 + 1) * 512],
                                       start=(kb == 0), stop=False)
                for ts4 in range(T // 512):
                    if evar != 1:
                        pe.matmul(e_ps[:, (pi * 4 + ts4) * 512:(pi * 4 + ts4 + 1) * 512],
                                  e_bcat[:, f0:f0 + 128],
                                  e_rb[:, ts4 * 512:(ts4 + 1) * 512], start=False, stop=False)
                    if evar != 2:
                        mm = pe.matmul(e_ps[:, (pi * 4 + ts4) * 512:(pi * 4 + ts4 + 1) * 512],
                                       c_fbias[0:1, f0:f0 + 128],
                                       c_onesb[0:1, 0:512], start=False, stop=True)
                    else:
                        mm = pe.matmul(e_ps[:, (pi * 4 + ts4) * 512:(pi * 4 + ts4 + 1) * 512],
                                       wsv[:, 31, :],
                                       x1r[:, 31 * T + ts4 * 512: 31 * T + (ts4 + 1) * 512],
                                       start=False, stop=True)
                S["e_mm"].inc(mm)
                P.consumed(("ews", pi), S["e_mm"])
                P.wait(ac, S["e_mm"])
                P.guard(ac, ("eev", pi))
                cp = None
                for ts4 in range(T // 512):
                    cp = ac.copy(e_ev[pi][:, ts4 * 512:(ts4 + 1) * 512],
                                 e_ps[:, (pi * 4 + ts4) * 512:(pi * 4 + ts4 + 1) * 512])
                S["e_ev"].inc(cp)
                P.consumed(("epq", pi), S["e_ev"])
                P.wait(sy, S["e_ev"])
                S["e_st" + str(pi)].inc(sy.dma_start(ffnT_d[f0:f0 + 128, :], e_ev[pi][:]), 16)
                P.consumed(("eev", pi), S["e_st" + str(pi)])

        if stop_after == "E":
            P.wait(sy, S["e_st0"])
            P.wait(sy, S["e_st1"])
            return nc
        for eng in (sy, dv, ac, pe):
            P.wait(eng, S["e_st0"])
            P.wait(eng, S["e_st1"])
        ln_phase(ffnT_d, None, outT, True, S["f_ld"], S["f_pe"], S["f_dv"], S["f_ac"],
                 S["f_st"], "f", None)
        P.wait(sy, S["f_st"])
    return nc


_CACHE = {}


def _host_consts():
    identf = np.eye(128, dtype=np.float32)
    identb = identf.astype(np.float32)
    s = np.arange(128)[:, None]
    t = np.arange(128)[None, :]
    ut01 = (s <= t).astype(np.float32)
    negut = np.where(s <= t, 0.0, -1e30).astype(np.float32)
    onesf = np.ones((1, 512), np.float32)
    onescf = np.ones((128, 1), np.float32)
    jrow = np.arange(3, dtype=np.float32)[None, :]
    cn = np.zeros((128, 8), np.float32)
    cn[:, CN_M1] = -1.0
    cn[:, CN_P1] = 1.0
    cn[:, CN_I512] = 1.0 / 512.0
    cn[:, CN_GNE] = 1e-5
    cn[:, CN_MH] = -0.5
    cn[:, CN_I4096] = 1.0 / 4096.0
    cn[:, CN_LNE] = 1e-5
    import ml_dtypes
    return dict(identf=identf, identb=identf.astype(ml_dtypes.bfloat16),
                ut01=ut01, negut=negut, onesf=onesf,
                onesb=onesf.astype(ml_dtypes.bfloat16),
                onescb=onescf.astype(ml_dtypes.bfloat16), jrow=jrow, cn=cn)


def kernel(L0_attn_hidden_states, L0_attn_in_proj_w, L0_attn_conv_weight,
           L0_attn_conv_bias, L0_attn_dt_bias, L0_attn_A_log, L0_attn_D,
           L0_attn_norm_weight, L0_attn_out_proj_w, L0_ffn_weight, L0_ffn_bias,
           L0_ffn_lora_A, L0_ffn_lora_B, L0_ffn_lora_start_indices):
    x = np.asarray(L0_attn_hidden_states)
    nb = x.shape[0]
    if "nc" not in _CACHE:
        _CACHE["nc"] = build_nc(dump=False)
    nc = _CACHE["nc"]

    cc = np.ascontiguousarray
    consts = _host_consts()
    winT = cc(np.asarray(L0_attn_in_proj_w).T)
    woutT = cc(np.asarray(L0_attn_out_proj_w).T)
    wffnT = cc(np.asarray(L0_ffn_weight).T)
    la = np.asarray(L0_ffn_lora_A)
    lb = np.asarray(L0_ffn_lora_B)
    acatT = cc(la.reshape(48, 4096).T)
    bcatT = cc(lb.transpose(0, 2, 1).reshape(48, 4096))
    convw_r = cc(np.asarray(L0_attn_conv_weight).reshape(48, 128, 4)
                 .transpose(1, 0, 2).reshape(128, 192))
    convb_r = cc(np.asarray(L0_attn_conv_bias).reshape(48, 128).T)
    nw_r = cc(np.asarray(L0_attn_norm_weight).reshape(32, 128).T)
    shared = dict(winT=winT, woutT=woutT, wffnT=wffnT, acatT=acatT, bcatT=bcatT,
                  fbias=cc(np.asarray(L0_ffn_bias)[None, :]),
                  convw_r=convw_r, convb_r=convb_r,
                  dtbias=cc(np.asarray(L0_attn_dt_bias)[:, None]),
                  alog=cc(np.asarray(L0_attn_A_log)[:, None]),
                  dvec=cc(np.asarray(L0_attn_D)[None, :]),
                  nw_r=nw_r,
                  lsi=cc(np.asarray(L0_ffn_lora_start_indices).astype(np.int32)[:, None]),
                  **consts)
    in_maps = []
    for b in range(nb):
        m = dict(shared)
        m["xT"] = cc(x[b].T)
        m["bid"] = np.array([[float(b)]], np.float32)
        in_maps.append(m)
    res = run_bass_kernel_spmd(nc, in_maps, core_ids=list(range(nb)))
    out = np.stack([np.asarray(res.results[b]["outT"]).T for b in range(nb)])
    return out
